# version 01 (final)
# baseline (speedup 1.0000x reference)
import math
import os
from contextlib import ExitStack

import numpy as np
import ml_dtypes

import concourse.bass as bass
import concourse.mybir as mybir
from concourse.bass_utils import run_bass_kernel_spmd

F32 = mybir.dt.float32
BF16 = mybir.dt.bfloat16
I32 = mybir.dt.int32
AF = mybir.ActivationFunctionType
ALU = mybir.AluOpType
BF = ml_dtypes.bfloat16

B, T, C, H, D = 4, 4096, 2048, 16, 128
INTER = 8192
NCTRL = 130156
NJ = 16777216
TPC = NJ // 16384 // 8
GRP, SCH = 16, 4
MYT = 2048
QBLK = [[0, 3, 4, 7], [1, 2, 5, 6]]
PADK = [8, 16, 24, 32]
SCALE = 1.0 / math.sqrt(128.0)
EPS = 1e-5

FWD_PHASES = int(os.environ.get("MK_FWD_PHASES", "5"))
FWD_DEBUG = set(os.environ.get("MK_FWD_DEBUG", "").split(",")) - {""}

_HOST_CACHE = {}


def spline_tables():
    if "spline" in _HOST_CACHE:
        return _HOST_CACHE["spline"]
    t = np.linspace(0.0, NCTRL - 1.0, NJ, dtype=np.float64)
    i = np.clip(np.floor(t).astype(np.int64), 0, NCTRL - 2)
    f = (t - i).astype(np.float32).astype(np.float64)
    f2 = f * f
    f3 = f2 * f
    c = np.empty((4, NJ), np.float64)
    c[0] = 0.5 * (-f + 2.0 * f2 - f3)
    c[1] = 0.5 * (2.0 - 5.0 * f2 + 3.0 * f3)
    c[2] = 0.5 * (f + 4.0 * f2 - 3.0 * f3)
    c[3] = 0.5 * (-f2 + f3)
    del f, f2, f3, t
    i_strip = i[::128]
    g = np.clip(i_strip - 1, 0, NCTRL - 5).astype(np.int64)
    gj = np.repeat(g, 128)
    d5 = np.zeros((5, NJ), np.float32)
    for k in range(4):
        idx = np.clip(i + k - 1, 0, NCTRL - 1)
        np.add.at(d5, (idx - gj, np.arange(NJ)), c[k].astype(np.float32))
    del c, i, gj
    NTILE = NJ // 16384
    d5 = d5.reshape(5, NTILE, 128, 128).transpose(1, 0, 2, 3).copy()
    gidx = g.reshape(NTILE, 128).astype(np.int32)
    _HOST_CACHE["spline"] = (d5, gidx)
    return d5, gidx


def host_masks():
    if "masks" in _HOST_CACHE:
        return _HOST_CACHE["masks"]
    out = np.zeros((2, 4, 8, 128, 512), dtype=BF)
    for pi in range(2):
        for j, blk in enumerate(QBLK[pi]):
            q_abs = blk * 512 + np.arange(512)
            nk = PADK[j]
            for m in range(8):
                k_abs = (nk - 8 + m) * 128 + np.arange(128)
                out[pi, j, m] = (k_abs[:, None] <= q_abs[None, :]).astype(BF)
    _HOST_CACHE["masks"] = out
    return out


def _zigzag_rows(pi):
    idx = []
    for blk in QBLK[pi]:
        idx.extend(range(blk * 512, (blk + 1) * 512))
    return np.array(idx)


class Ctx:
    ENGS = ("tensor", "vector", "scalar", "gpsimd", "sync")

    def __init__(self, alias=None):
        self.cmds = {e: [] for e in self.ENGS}
        self.counts = {}
        self.alias = alias or {}

    def m(self, name):
        return self.alias.get(name, name)

    def n(self, name):
        return self.counts.get(self.m(name), 0)

    def inc(self, name, k):
        name = self.m(name)
        self.counts[name] = self.counts.get(name, 0) + k
        return self.counts[name]

    def op(self, eng, fn, incs=(), waits=()):
        wl = tuple((self.m(nm), v) for nm, v in waits)
        il = []
        for nm, k in incs:
            self.inc(nm, k)
            il.append((self.m(nm), k))
        il = tuple(il)

        def cmd(e, S, wl=wl, il=il, fn=fn):
            for nm, val in wl:
                e.wait_ge(S[nm], val)
            inst = fn(e, S) if fn is not None else None
            for nm, k in il:
                inst.then_inc(S[nm], k)
        self.cmds[eng].append(cmd)


def build_gen():
    nc = bass.Bass("TRN2", target_bir_lowering=False, debug=False)
    cps = [nc.dram_tensor(n, [NCTRL], F32, kind="ExternalInput").ap()
           for n in ("gcp", "ucp", "dcp")]
    d5 = nc.dram_tensor("d5s", [TPC, 5, 128, 128], F32, kind="ExternalInput").ap()
    gidxT = nc.dram_tensor("gidxT", [128, TPC], I32, kind="ExternalInput").ap()
    wouts = [nc.dram_tensor(n, [TPC * 16384], BF16, kind="ExternalOutput").ap()
             for n in ("wg", "wu", "wd")]
    ND5 = 4
    NT = 3 * TPC
    with ExitStack() as ctx:
        E = ctx.enter_context
        d5_sb = E(nc.sbuf_tensor("d5_sb", [128, ND5 * 5 * 128], F32))
        p5_sb = E(nc.sbuf_tensor("p5_sb", [128, NT * 5], F32))
        gidx_sb = E(nc.sbuf_tensor("gidx_sb", [128, TPC], I32))
        acc_sb = E(nc.sbuf_tensor("acc_sb", [128, 128], F32))
        fl_sb = E(nc.sbuf_tensor("fl_sb", [1, 8], BF16))
        wbuf = E(nc.sbuf_tensor("wbuf", [128, 2 * GRP * 128], BF16))
        s_d5 = [E(nc.semaphore(f"s_d5_{i}")) for i in range(ND5)]
        s_st = [E(nc.semaphore(f"s_st_{i}")) for i in range(2)]
        s_gx = E(nc.semaphore("s_gx"))
        s_p5 = E(nc.semaphore("s_p5"))
        s_dve = E(nc.semaphore("s_dve"))
        s_wb = E(nc.semaphore("s_wb"))
        block = E(nc.Block())

        @block.sync
        def _(sync):
            sync.dma_start(out=gidx_sb[:, :], in_=gidxT).then_inc(s_gx, 16)
            for it in range(NT):
                t = it % TPC
                buf = it % ND5
                if it >= ND5:
                    sync.wait_ge(s_dve, it - ND5 + 1)
                sync.dma_start(
                    out=d5_sb.ap().rearrange("p (b m f) -> p b m f", b=ND5, m=5)[:, buf],
                    in_=d5[t].transpose([1, 0, 2]),
                ).then_inc(s_d5[buf], 16)

        @block.gpsimd
        def _(gpsimd):
            gpsimd.wait_ge(s_gx, 16)
            for w in range(3):
                for t in range(TPC):
                    it = w * TPC + t
                    gpsimd.indirect_dma_start(
                        out=p5_sb[:, it * 5:(it + 1) * 5],
                        out_offset=None,
                        in_=cps[w].rearrange("(n o) -> n o", o=1),
                        in_offset=bass.IndirectOffsetOnAxis(ap=gidx_sb[:, t:t + 1], axis=0),
                    ).then_inc(s_p5, 16)
            nchunk = GRP // SCH
            for ig in range(NT // GRP):
                w, g = divmod(ig, TPC // GRP)
                gb = ig % 2
                for ch in range(nchunk):
                    gpsimd.wait_ge(s_wb, ig * nchunk + ch + 1)
                    g0 = g * GRP + ch * SCH
                    gpsimd.dma_start(
                        out=wouts[w].rearrange("(a b) -> a b", b=128)
                            .rearrange("(g s) b -> s g b", s=128)[:, g0:g0 + SCH, :],
                        in_=wbuf.ap().rearrange("p (c g f) -> p c g f", c=2, g=GRP)
                            [:, gb, ch * SCH:(ch + 1) * SCH],
                    ).then_inc(s_st[gb], 16)
            for gb in range(2):
                n = sum(1 for ig in range(NT // GRP) if ig % 2 == gb) * nchunk
                gpsimd.wait_ge(s_st[gb], 16 * n)

        @block.vector
        def _(vector):
            nchunk = GRP // SCH
            vector.wait_ge(s_p5, 16 * NT)
            for it in range(NT):
                buf = it % ND5
                ig, gt = divmod(it, GRP)
                gb = ig % 2
                vector.wait_ge(s_d5[buf], 16 * (it // ND5 + 1))
                if gt == 0 and ig >= 2:
                    vector.wait_ge(s_st[gb], 16 * (ig // 2) * nchunk)
                d5v = d5_sb.ap().rearrange("p (b m f) -> p b m f", b=ND5, m=5)[:, buf]
                p5v = p5_sb[:, it * 5:(it + 1) * 5]
                vector.scalar_tensor_tensor(acc_sb[:, :], d5v[:, 0], p5v[:, 0:1],
                                            acc_sb[:, :], ALU.mult, ALU.bypass)
                for m in range(1, 4):
                    vector.scalar_tensor_tensor(acc_sb[:, :], d5v[:, m], p5v[:, m:m + 1],
                                                acc_sb[:, :], ALU.mult, ALU.add)
                o = wbuf[:, (gb * GRP + gt) * 128:(gb * GRP + gt + 1) * 128]
                vector.scalar_tensor_tensor(o, d5v[:, 4], p5v[:, 4:5],
                                            acc_sb[:, :], ALU.mult, ALU.add
                                            ).then_inc(s_dve, 1)
                if gt % SCH == SCH - 1:
                    vector.tensor_copy(fl_sb[0:1, 0:1], o[0:1, 127:128]).then_inc(s_wb, 1)
    return nc


def build_fwd():
    dbg = FWD_DEBUG
    nph = FWD_PHASES
    nc = bass.Bass("TRN2", target_bir_lowering=False, debug=False)

    xk = nc.dram_tensor("xk", [T, C], F32, kind="ExternalInput").ap()
    xq = nc.dram_tensor("xq", [MYT, C], F32, kind="ExternalInput").ap()
    wqkv = nc.dram_tensor("wqkv", [3, H, C, D], F32, kind="ExternalInput").ap()
    wo = nc.dram_tensor("wo", [C, C], F32, kind="ExternalInput").ap()
    bqkv = nc.dram_tensor("bqkv", [3, 1, H * D], F32, kind="ExternalInput").ap()
    bo = nc.dram_tensor("bo", [1, C], F32, kind="ExternalInput").ap()
    wgT = nc.dram_tensor("wgT", [C, INTER], BF16, kind="ExternalInput").ap()
    wuT = nc.dram_tensor("wuT", [C, INTER], BF16, kind="ExternalInput").ap()
    wdT = nc.dram_tensor("wdT", [INTER, C], BF16, kind="ExternalInput").ap()
    masks = nc.dram_tensor("masks", [4, 8, 128, 512], BF16, kind="ExternalInput").ap()
    identf = nc.dram_tensor("identf", [128, 128], F32, kind="ExternalInput").ap()
    onesf = nc.dram_tensor("onesf", [1, 128], F32, kind="ExternalInput").ap()
    onesb = nc.dram_tensor("onesb", [1, 128], BF16, kind="ExternalInput").ap()
    onescol = nc.dram_tensor("onescol", [128, 1], BF16, kind="ExternalInput").ap()

    def scratch(name, shape, dt):
        kind = "ExternalOutput" if name in dbg else "Internal"
        return nc.dram_tensor(name, shape, dt, kind=kind).ap()

    out = nc.dram_tensor("out", [MYT, C], F32, kind="ExternalOutput").ap()
    xk_bf = scratch("xk_bf", [T, C], BF16)
    xq_bf = scratch("xq_bf", [MYT, C], BF16)
    rows_d = scratch("rows_d", [4, 16384], F32)
    kf_d = scratch("kf_d", [H, D, T], BF16)
    vt_d = scratch("vt_d", [T, C], BF16)
    qf_d = scratch("qf_d", [H, D, MYT], BF16)
    yall_d = scratch("yall_d", [128, H * MYT], BF16)
    x1_d = scratch("x1_d", [MYT, C], F32)
    ln2_d = scratch("ln2_d", [MYT, C], BF16)
    dbg_d = scratch("dbg_d", [1, 64], F32)
    attn_d = scratch("attn_d", [MYT, C], BF16)
    ffn_d = scratch("ffn_d", [MYT, C], BF16)

    alias = {
        "d_cmv": "a_ve", "bb_ve": "a_ve",
        "d_cma": "a_act", "bb_act": "a_act", "d_lnav": "a_act", "e_silu": "a_act",
        "d_ev": "a_sq", "e_ev": "a_sq", "d_xsq": "a_sq",
        "e_ss": "d_ss",
        "bb_mm": "a_tr", "c_rbmm": "b_rbmm",
        "bb_po": "c_msk", "e_int": "c_msk",
        "d_pe": "b_pe", "e_pe": "b_pe", "e_dmm": "b_pe",
        "e_up": "b_ev", "e_fin": "b_ev", "d_x1": "b_ev",
        "d_cm2": "d_cm", "e_cm": "d_cm",
        "e_ln": "c_mld",
        "d_xq0": "b_rr0", "d_xq1": "b_rr1",
        "d_atl0": "c_kh0", "d_atl1": "c_kh1",
        "d_x1l0": "c_qh0", "d_x1l1": "c_qh1",
        "e_x1l0": "c_kh0", "e_x1l1": "c_kh1",
        "e_ffl0": "c_vh0", "e_ffl1": "c_vh1",
        "e_wg0": "b_sl0", "e_wg1": "b_sl1",
        "e_wu0": "c_qh0", "e_wu1": "c_qh1",
        "e_wd0": "a_ld0", "e_wd1": "a_ld1", "e_wd2": "a_ld2", "e_wd3": "a_ld3",
        "d_at0": "b_ko0", "d_at1": "b_ko1",
        "d_x1s0": "b_ko2", "d_x1s1": "b_ko3",
        "d_ln0": "b_ko0", "d_ln1": "b_ko1",
        "e_ffs0": "b_ko0", "e_ffs1": "b_ko1", "e_ffs2": "b_ko2", "e_ffs3": "b_ko3",
        "e_out0": "b_ko0", "e_out1": "b_ko1",
    }
    cx = Ctx(alias)

    with ExitStack() as ctx:
        E = ctx.enter_context

        def sb(name, shape, dt):
            return E(nc.sbuf_tensor(name, shape, dt))

        W_sb = sb("W_sb", [128, 32768], BF16)
        YB = sb("YB", [128, 32768], BF16)
        IB = sb("IB", [128, 24576], BF16)
        identsb = sb("identsb", [128, 128], F32)
        stats = sb("stats", [128, 1024], F32)
        stag = sb("stag", [128, 512], F32)
        onestile = sb("onestile", [128, 32], F32)
        onescol_sb = sb("onescol_sb", [128, 1], BF16)
        onesrow_sb = sb("onesrow_sb", [1, 128], BF16)
        onesfrow = sb("onesfrow", [1, 128], F32)
        bmax = sb("bmax", [1, 256], F32)
        biascol = sb("biascol", [128, 1], F32)
        sumrow = sb("sumrow", [1, 512], F32)
        rbq = sb("rbq", [128, 512], F32)
        colw = sb("colw", [128, 64], F32)
        bo_sb = sb("bo_sb", [1, 2048], BF16)
        scrw = sb("scrw", [1, 16], F32)

        psum = [E(nc.psum_tensor(f"ps{i}", [128, 512], F32)) for i in range(8)]

        SEM_NAMES = set()

        def S_(name):
            SEM_NAMES.add(cx.m(name))
            return name

        cx.op("sync", lambda e, S: e.dma_start(out=identsb[:, :], in_=identf),
              incs=[(S_("set_ld"), 16)])
        cx.op("sync", lambda e, S: e.dma_start(out=onesfrow[:, :], in_=onesf),
              incs=[("set_ld", 16)])
        cx.op("sync", lambda e, S: e.dma_start(out=onesrow_sb[:, :], in_=onesb),
              incs=[("set_ld", 16)])
        cx.op("sync", lambda e, S: e.dma_start(out=onescol_sb[:, :], in_=onescol),
              incs=[("set_ld", 16)])
        SET_LD = cx.n("set_ld")
        cx.op("vector", lambda e, S: e.memset(onestile[:, :], 1.0),
              incs=[(S_("set_ve"), 1)])
        cx.op("vector", lambda e, S: e.memset(bmax[:, :], 0.0),
              incs=[("set_ve", 1)])
        cx.op("vector", lambda e, S: e.memset(YB[0:2, 10240:12288], 0.0),
              incs=[("set_ve", 1)])
        cx.op("vector", lambda e, S: e.memset(scrw[:, :], 0.0),
              incs=[("set_ve", 1)])
        cx.op("vector", lambda e, S: e.memset(stats[:, :], 0.0),
              incs=[("set_ve", 1)])
        SET_VE = cx.n("set_ve")

        xa = IB.ap().bitcast(F32)
        xbr = YB
        scr_a = IB[:, 16384:18432]

        def phase_a(src, n_tiles, dst_bf, sum_base, gt0):
            for t in range(n_tiles):
                gt = gt0 + t
                slv = gt % 4
                ldsem = S_("a_ld%d" % slv)
                xa_t = xa[:, slv * 2048:(slv + 1) * 2048]
                xb_t = xbr[:, slv * 2048:(slv + 1) * 2048]
                w = [("a_sq", cx.n("a_sq") - 3)] if gt >= 4 else []
                cx.op("sync", lambda e, S, s=src, t=t, d=xa_t: e.dma_start(
                    out=d, in_=s[t * 128:(t + 1) * 128, :]),
                    incs=[(ldsem, 16)], waits=w)
                ldv = cx.n(ldsem)
                cx.op("scalar", lambda e, S, i=xa_t, o=xb_t,
                      a=stats[:, sum_base + t:sum_base + t + 1]:
                      e.activation(out=o, in_=i, func=AF.Copy, accum_out=a),
                      incs=[(S_("a_act"), 1)],
                      waits=[(ldsem, ldv)] + ([("set_ve", cx.n("set_ve"))]
                                             if gt == 0 else []))
                cx.op("scalar", lambda e, S, i=xa_t, o=scr_a[:, 0:2048],
                      a=stats[:, sum_base + 128 + t:sum_base + 129 + t]:
                      e.activation(out=o, in_=i, func=AF.Square, accum_out=a),
                      incs=[(S_("a_sq"), 1)])
                cx.op("gpsimd", lambda e, S, o=dst_bf, t=t, i=xb_t: e.dma_start(
                    out=o[t * 128:(t + 1) * 128, :], in_=i),
                    incs=[(S_("a_st"), 16)], waits=[("a_act", cx.n("a_act"))])

        phase_a(xk, 32, xk_bf, 0, 0)
        phase_a(xq, 16, xq_bf, 32, 32)
        A_ACT, A_SQ, A_ST = cx.n("a_act"), cx.n("a_sq"), cx.n("a_st")

        def col_math():
            mu = stats[:, 256:384]
            sd = stats[:, 384:512]
            nmu = stats[:, 512:640]
            rr = stats[:, 640:768]
            tmp = stats[:, 768:896]
            s_sum = stats[:, 0:128]
            s_sq = stats[:, 128:256]
            cx.op("vector", lambda e, S: e.scalar_tensor_tensor(
                mu, s_sum, 1.0 / C, s_sum, ALU.mult, ALU.bypass),
                waits=[("a_act", A_ACT), ("a_sq", A_SQ)])
            cx.op("vector", lambda e, S: e.scalar_tensor_tensor(
                tmp, s_sq, 1.0 / C, s_sq, ALU.mult, ALU.bypass),
                waits=[("a_act", cx.n("a_act"))])
            cx.op("vector", lambda e, S: e.tensor_tensor(nmu, mu, mu, ALU.mult))
            cx.op("vector", lambda e, S: e.tensor_tensor(tmp, tmp, nmu, ALU.subtract))
            cx.op("vector", lambda e, S: e.scalar_tensor_tensor(
                tmp, tmp, EPS, tmp, ALU.add, ALU.bypass),
                incs=[(S_("a_ve"), 1)])
            lnv = stats[:, 896:1024]
            h1 = stag[:, 0:128]
            cx.op("scalar", lambda e, S: e.activation(out=lnv, in_=tmp, func=AF.Ln),
                  incs=[("a_act", 1)], waits=[("a_ve", cx.n("a_ve"))])
            cx.op("vector", lambda e, S: e.scalar_tensor_tensor(
                h1, lnv, 0.5, lnv, ALU.mult, ALU.bypass),
                waits=[("a_act", cx.n("a_act"))])
            cx.op("vector", lambda e, S: e.scalar_tensor_tensor(
                lnv, lnv, -0.5, lnv, ALU.mult, ALU.bypass),
                incs=[("a_ve", 1)])
            cx.op("scalar", lambda e, S: e.activation(out=sd, in_=h1, func=AF.Exp),
                  waits=[("a_ve", cx.n("a_ve"))])
            cx.op("scalar", lambda e, S: e.activation(out=rr, in_=lnv, func=AF.Exp),
                  incs=[("a_act", 1)])
            cx.op("vector", lambda e, S: e.tensor_tensor(h1, rr, rr, ALU.mult),
                  waits=[("a_act", cx.n("a_act"))])
            cx.op("vector", lambda e, S: e.tensor_tensor(h1, h1, tmp, ALU.mult))
            cx.op("vector", lambda e, S: e.scalar_tensor_tensor(
                h1, h1, -0.5, h1, ALU.mult, ALU.bypass))
            cx.op("vector", lambda e, S: e.scalar_tensor_tensor(
                h1, h1, 1.5, h1, ALU.add, ALU.bypass))
            cx.op("vector", lambda e, S: e.tensor_tensor(rr, rr, h1, ALU.mult))
            cx.op("vector", lambda e, S: e.scalar_tensor_tensor(
                nmu, mu, -1.0, mu, ALU.mult, ALU.bypass), incs=[("a_ve", 1)])

        col_math()
        A_VE = cx.n("a_ve")

        for si, cb in enumerate((256, 384, 512, 640)):
            cx.op("tensor", lambda e, S, cb=cb: e.transpose(
                psum[0][:, 0:128], stats[:, cb:cb + 128], identsb[:, :]),
                incs=[(S_("a_tr"), 1)],
                waits=[("a_ve", A_VE), ("set_ld", SET_LD),
                       (S_("a_tev"), cx.n("a_tev"))])
            cx.op("vector", lambda e, S, si=si: e.tensor_copy(
                stag[:, si * 128:(si + 1) * 128], psum[0][:, 0:128]),
                incs=[(S_("a_tev"), 1)], waits=[("a_tr", cx.n("a_tr"))])
        cx.op("gpsimd", lambda e, S: e.dma_start(
            out=rows_d.rearrange("a (p f) -> p a f", p=128),
            in_=stag[:, 0:512].rearrange("p (a f) -> p a f", a=4)),
            incs=[(S_("a_bnc"), 16)], waits=[("a_tev", cx.n("a_tev"))])
        A_BNC = cx.n("a_bnc")

        musd_a = YB[0:2, 0:4096]
        musd_q = YB[0:2, 4096:6144]
        vcl = YB[0:2, 6144:10240]
        corr = YB[0:2, 10240:12288]
        cx.op("gpsimd", lambda e, S: e.dma_start(
            out=musd_a, in_=rows_d[0:2, 0:4096]),
            incs=[(S_("rows_ld"), 16)], waits=[("a_bnc", A_BNC), ("a_st", A_ST)])
        cx.op("gpsimd", lambda e, S: e.dma_start(
            out=musd_q, in_=rows_d[0:2, 4096:6144]),
            incs=[("rows_ld", 16)])
        cx.op("gpsimd", lambda e, S: e.dma_start(
            out=vcl[0:1, :], in_=rows_d[2:3, 0:4096]),
            incs=[("rows_ld", 16)])
        cx.op("gpsimd", lambda e, S: e.dma_start(
            out=vcl[1:2, :], in_=rows_d[1:2, 0:4096]),
            incs=[("rows_ld", 16)])
        ROWS_LD = cx.n("rows_ld")

        if nph < 2:
            _finish(cx)
            _emit(nc, cx, SEM_NAMES, ctx)
            return nc

        slab = IB
        kout = IB[:, 18432:20480]
        rrowr = IB.ap().bitcast(F32)[0:1, 10240:11264]
        kuse = {0: 0, 1: 0, 2: 0, 3: 0}
        kolast = {0: 0, 1: 0, 2: 0, 3: 0}

        cx.op("sync", None, waits=[("a_st", A_ST), ("a_sq", A_SQ),
                                   ("a_act", A_ACT)])
        cx.op("tensor", None, waits=[("a_tev", 2)])

        def b_pass(m, src_bf, nt_tok, musd, kind):
            dst = {"K": kf_d, "Q": qf_d, "V": vt_d}[kind]
            B_PE0 = cx.n("b_pe") if "b_pe" in cx.counts else 0
            for cc in range(16):
                cx.op("gpsimd", lambda e, S, m=m, cc=cc: e.dma_start(
                    out=W_sb[:, cc * 2048:(cc + 1) * 2048],
                    in_=wqkv[m, :, cc * 128:(cc + 1) * 128, :].transpose([1, 0, 2])),
                    incs=[(S_("w_ld"), 16)],
                    waits=([(S_("b_pe"), B_PE0)] if cc == 0 and B_PE0 else []))
            W_LD = cx.n("w_ld")
            for wdw in range(4):
                for cc in range(16):
                    cx.op("tensor", lambda e, S, wdw=wdw, cc=cc: e.matmul(
                        psum[6][0:1, :], onescol_sb[:, :],
                        W_sb[:, cc * 2048 + wdw * 512:cc * 2048 + (wdw + 1) * 512],
                        start=(cc == 0), stop=(cc == 15)),
                        incs=([(S_("b_ws"), 1)] if cc == 15 else []),
                        waits=([("w_ld", W_LD), ("set_ld", SET_LD),
                                (S_("b_cev"), cx.n("b_cev"))] if cc == 0 else []))
                sgn = 1.0 if kind == "V" else -1.0
                cx.op("vector", lambda e, S, wdw=wdw, sgn=sgn: e.scalar_tensor_tensor(
                    corr[0:1, wdw * 512:(wdw + 1) * 512], psum[6][0:1, :],
                    sgn, corr[0:1, wdw * 512:(wdw + 1) * 512], ALU.mult, ALU.bypass),
                    incs=[(S_("b_cev"), 1)],
                    waits=[("b_ws", cx.n("b_ws")), ("rows_ld", ROWS_LD)])
            cx.op("gpsimd", lambda e, S, m=m: e.dma_start(
                out=corr[1:2, :], in_=bqkv[m]),
                incs=[(S_("b_bld"), 16)],
                waits=[("rows_ld", ROWS_LD), ("b_pe", B_PE0)] if B_PE0 else
                      [("rows_ld", ROWS_LD)])
            B_CEV, B_BLD = cx.n("b_cev"), cx.n("b_bld")

            if kind in ("K", "Q"):
                rbase = 0 if kind == "K" else 4096
                for tb in range(nt_tok // 512):
                    slv = tb % 2
                    sls = S_("b_sl%d" % slv)
                    rsl = S_("b_rr%d" % slv)
                    w = [("b_pe", cx.n("b_pe") - 16)] if cx.n("b_pe") >= 16 else []
                    cx.op("sync", lambda e, S, rbase=rbase, tb=tb, slv=slv:
                          e.dma_start(out=rrowr[0:1, slv * 512:(slv + 1) * 512],
                                      in_=rows_d[3:4,
                                                 rbase + tb * 512:rbase + (tb + 1) * 512]),
                          incs=[(rsl, 16)], waits=w + [("a_bnc", A_BNC)])
                    for cc in range(16):
                        cx.op("sync", lambda e, S, src=src_bf, tb=tb, cc=cc, slv=slv:
                              e.dma_start_transpose(
                                  slab[:, slv * 8192 + cc * 512:slv * 8192 + (cc + 1) * 512],
                                  src[tb * 512:(tb + 1) * 512, cc * 128:(cc + 1) * 128]),
                              incs=[(sls, 16)])
                    SLS, RSL = cx.n(sls), cx.n(rsl)
                    cx.op("tensor", lambda e, S, slv=slv: e.matmul(
                        psum[7][:, :], onesfrow[:, :],
                        rrowr[0:1, slv * 512:(slv + 1) * 512],
                        start=True, stop=True),
                        incs=[(S_("b_rbmm"), 1)],
                        waits=[(rsl, RSL), (S_("b_rbev"), cx.n("b_rbev"))])
                    cx.op("vector", lambda e, S: e.tensor_copy(rbq[:, :], psum[7][:, :]),
                          incs=[(S_("b_rbev"), 1)], waits=[("b_rbmm", cx.n("b_rbmm"))])
                    B_RBEV = cx.n("b_rbev")
                    for h in range(H):
                        bank = h % 2
                        wts = [(sls, SLS), ("b_bld", B_BLD), ("b_cev", B_CEV)]
                        if cx.n("b_ev") >= 2:
                            wts.append((S_("b_ev"), cx.n("b_ev") - 1))
                        for cc in range(16):
                            cx.op("tensor", lambda e, S, bank=bank, h=h, cc=cc, slv=slv:
                                  e.matmul(psum[bank][:, :],
                                           W_sb[:, cc * 2048 + h * 128:cc * 2048 + (h + 1) * 128],
                                           slab[:, slv * 8192 + cc * 512:slv * 8192 + (cc + 1) * 512],
                                           start=(cc == 0), stop=False),
                                  waits=(wts if cc == 0 else []))
                        cx.op("tensor", lambda e, S, bank=bank, h=h, tb=tb, musd=musd:
                              e.matmul(psum[bank][:, :], corr[:, h * 128:(h + 1) * 128],
                                       musd[:, tb * 512:(tb + 1) * 512],
                                       start=False, stop=True),
                              incs=[("b_pe", 1)])
                        B_PE = cx.n("b_pe")
                        ks = h % 4
                        ko = kout[:, ks * 512:(ks + 1) * 512]
                        kow = ([("b_ko%d" % ks, kolast[ks])] if kuse[ks] else [])
                        cx.op("vector", lambda e, S, bank=bank, ko=ko: e.tensor_tensor(
                            ko, psum[bank][:, :], rbq[:, :], ALU.mult),
                            incs=[(S_("b_ev"), 1)],
                            waits=[("b_pe", B_PE), ("b_rbev", B_RBEV)] + kow)
                        B_EV = cx.n("b_ev")
                        cx.op("scalar", lambda e, S, ko=ko: e.activation(
                            out=scr_a[:, 0:512], in_=ko, func=AF.Square),
                            incs=[(S_("b_n2"), 1)],
                            waits=[("b_ev", B_EV), (S_("b_nmm"), cx.n("b_nmm"))])
                        cx.op("tensor", lambda e, S: e.matmul(
                            psum[5][0:1, :], onescol_sb[:, :], scr_a[:, 0:512],
                            start=True, stop=True),
                            incs=[(S_("b_nmm"), 1)],
                            waits=[("b_n2", cx.n("b_n2")),
                                   (S_("b_nred"), cx.n("b_nred"))])
                        slot = (0 if kind == "K" else 128) + (tb * H + h) % 128
                        cx.op("vector", lambda e, S, slot=slot: e.reduce_max(
                            bmax[0:1, slot:slot + 1], psum[5][0:1, :],
                            mybir.AxisListType.X),
                            incs=[("b_nred", 1)],
                            waits=[("b_nmm", cx.n("b_nmm")), ("set_ve", SET_VE)])
                        cx.op("gpsimd", lambda e, S, dstd=dst, h=h, tb=tb, ko=ko:
                              e.dma_start(out=dstd[h, :, tb * 512:(tb + 1) * 512], in_=ko),
                              incs=[(S_("b_ko%d" % ks), 16)], waits=[("b_ev", B_EV)])
                        kuse[ks] += 1
                        kolast[ks] = cx.n("b_ko%d" % ks)
            else:
                for tt in range(32):
                    slv = (tt // 4) % 2
                    sls = S_("b_sl%d" % slv)
                    if tt % 4 == 0:
                        tb = tt // 4
                        w = [("b_pe", cx.n("b_pe") - 16)] if cx.n("b_pe") >= 16 else []
                        for cc in range(16):
                            cx.op("sync", lambda e, S, tb=tb, cc=cc, slv=slv:
                                  e.dma_start_transpose(
                                      slab[:, slv * 8192 + cc * 512:slv * 8192 + (cc + 1) * 512],
                                      xk_bf[tb * 512:(tb + 1) * 512, cc * 128:(cc + 1) * 128]),
                                  incs=[(sls, 16)], waits=(w if cc == 0 else []))
                    SLS = cx.n(sls)
                    for hg in range(4):
                        bank = (tt * 4 + hg) % 2
                        wts = [(sls, SLS), ("b_bld", B_BLD), ("b_cev", B_CEV),
                               ("rows_ld", ROWS_LD)]
                        if cx.n("b_ev") >= 2:
                            wts.append(("b_ev", cx.n("b_ev") - 1))
                        for cc in range(16):
                            cx.op("tensor", lambda e, S, bank=bank, tt=tt, hg=hg, cc=cc, slv=slv:
                                  e.matmul(psum[bank][:, :],
                                           slab[:, slv * 8192 + cc * 512 + (tt % 4) * 128:
                                                slv * 8192 + cc * 512 + (tt % 4 + 1) * 128],
                                           W_sb[:, cc * 2048 + hg * 512:cc * 2048 + (hg + 1) * 512],
                                           start=(cc == 0), stop=False),
                                  waits=(wts if cc == 0 else []))
                        cx.op("tensor", lambda e, S, bank=bank, tt=tt, hg=hg: e.matmul(
                            psum[bank][:, :], vcl[:, tt * 128:(tt + 1) * 128],
                            corr[:, hg * 512:(hg + 1) * 512],
                            start=False, stop=True),
                            incs=[("b_pe", 1)])
                        B_PE = cx.n("b_pe")
                        ks = (tt * 4 + hg) % 4
                        ko = kout[:, ks * 512:(ks + 1) * 512]
                        kow = ([("b_ko%d" % ks, kolast[ks])] if kuse[ks] else [])
                        cx.op("vector", lambda e, S, bank=bank, ko=ko, tt=tt:
                              e.tensor_scalar(ko, psum[bank][:, :],
                                              stats[:, 640 + tt:641 + tt], None, ALU.mult),
                              incs=[("b_ev", 1)], waits=[("b_pe", B_PE)] + kow)
                        B_EV = cx.n("b_ev")
                        cx.op("gpsimd", lambda e, S, tt=tt, hg=hg, ko=ko:
                              e.dma_start(out=vt_d[tt * 128:(tt + 1) * 128,
                                                   hg * 512:(hg + 1) * 512], in_=ko),
                              incs=[(S_("b_ko%d" % ks), 16)], waits=[("b_ev", B_EV)])
                        kuse[ks] += 1
                        kolast[ks] = cx.n("b_ko%d" % ks)

        b_pass(1, xk_bf, T, musd_a, "K")
        b_pass(0, xq_bf, MYT, musd_q, "Q")
        b_pass(2, xk_bf, T, None, "V")

        cx.op("vector", lambda e, S: e.reduce_max(
            scrw[0:1, 0:1], bmax[0:1, 0:128], mybir.AxisListType.X),
            waits=[("b_nred", cx.n("b_nred"))])
        cx.op("vector", lambda e, S: e.reduce_max(
            scrw[0:1, 1:2], bmax[0:1, 128:256], mybir.AxisListType.X),
            incs=[(S_("bb_ve"), 1)])
        cx.op("gpsimd", lambda e, S: e.tensor_tensor(
            scrw[0:1, 2:3], scrw[0:1, 0:1], scrw[0:1, 1:2], ALU.mult),
            incs=[(S_("bb_po"), 1)], waits=[("bb_ve", cx.n("bb_ve"))])
        cx.op("scalar", lambda e, S: e.activation(
            out=scrw[0:1, 3:4], in_=scrw[0:1, 2:3], func=AF.Ln),
            incs=[(S_("bb_act"), 1)], waits=[("bb_po", cx.n("bb_po"))])
        cx.op("vector", lambda e, S: e.scalar_tensor_tensor(
            scrw[0:1, 5:6], scrw[0:1, 3:4], 0.5, scrw[0:1, 3:4],
            ALU.mult, ALU.bypass),
            incs=[("bb_ve", 1)], waits=[("bb_act", cx.n("bb_act"))])
        cx.op("scalar", lambda e, S: e.activation(
            out=scrw[0:1, 6:7], in_=scrw[0:1, 5:6], func=AF.Exp),
            incs=[("bb_act", 1)], waits=[("bb_ve", cx.n("bb_ve"))])
        cx.op("vector", lambda e, S: e.scalar_tensor_tensor(
            scrw[0:1, 4:5], scrw[0:1, 6:7], -SCALE, scrw[0:1, 6:7],
            ALU.mult, ALU.bypass),
            incs=[("bb_ve", 1)], waits=[("bb_act", cx.n("bb_act"))])
        cx.op("tensor", lambda e, S: e.matmul(
            psum[7][:, 0:1], onesfrow[:, :], scrw[0:1, 4:5], start=True, stop=True),
            incs=[(S_("bb_mm"), 1)],
            waits=[("bb_ve", cx.n("bb_ve")), ("b_rbev", cx.n("b_rbev"))])
        cx.op("vector", lambda e, S: e.tensor_copy(biascol[:, :], psum[7][:, 0:1]),
              incs=[("bb_ve", 1)], waits=[("bb_mm", cx.n("bb_mm"))])
        cx.op("gpsimd", lambda e, S: e.dma_start(out=dbg_d[0:1, 0:8], in_=scrw[0:1, 0:8]),
              incs=[(S_("dbg_st"), 16)], waits=[("bb_ve", cx.n("bb_ve"))])

        if nph < 3:
            _finish(cx)
            _emit(nc, cx, SEM_NAMES, ctx)
            return nc

        mask_c = IB
        probs = IB[:, 16384:18432]
        BBVE = cx.n("bb_ve")
        B_PE_ALL = cx.n("b_pe")

        S_("c_ynorm")
        cx.op("sync", None, waits=[("b_pe", B_PE_ALL), ("b_ev", cx.n("b_ev")),
                                   ("b_nmm", cx.n("b_nmm"))])
        cx.op("sync", lambda e, S: e.dma_start(
            out=mask_c.ap().rearrange("p (t f) -> p t f", f=512)[:, 0:32],
            in_=masks.rearrange("a b p f -> p (a b) f")),
            incs=[(S_("c_mld"), 16)])
        C_MLD = cx.n("c_mld")

        khv = []
        for h in range(H):
            slv = h % 2
            wts = [(S_("c_ynorm"), 4 * (h - 1))] if h >= 2 else []
            cx.op("sync", lambda e, S, h=h, slv=slv: e.dma_start(
                out=W_sb[:, slv * 4096:(slv + 1) * 4096], in_=kf_d[h]),
                incs=[(S_("c_kh%d" % slv), 16)],
                waits=wts + [("b_ko%d" % i, cx.n("b_ko%d" % i)) for i in range(4)])
            cx.op("sync", lambda e, S, h=h, slv=slv: e.dma_start(
                out=W_sb[:, 8192 + slv * 4096:8192 + (slv + 1) * 4096]
                    .rearrange("p (kc d) -> p kc d", d=128),
                in_=vt_d.rearrange("(kc p) c -> p kc c", p=128)[:, :, h * 128:(h + 1) * 128]),
                incs=[(S_("c_vh%d" % slv), 16)])
            cx.op("sync", lambda e, S, h=h, slv=slv: e.dma_start(
                out=W_sb[:, 16384 + slv * 2048:16384 + (slv + 1) * 2048], in_=qf_d[h]),
                incs=[(S_("c_qh%d" % slv), 16)])
            khv.append((cx.n("c_kh%d" % slv), cx.n("c_vh%d" % slv), cx.n("c_qh%d" % slv)))

        tiles = []
        for h in range(H):
            for qb in range(4):
                nk = PADK[qb]
                for kc in range(nk):
                    tiles.append((h, qb, kc, nk))
        NTL = len(tiles)
        mcount = 0
        qtail = 0

        def pe_score(i):
            h, qb, kc, nk = tiles[i]
            slv = h % 2
            bank = i % 2
            wts = []
            if i >= 2:
                wts.append((S_("c_exp"), i - 1))
            KH, VH, QH = khv[h]
            wts += [("c_kh%d" % slv, KH), ("c_qh%d" % slv, QH)]
            cx.op("tensor", lambda e, S, slv=slv, bank=bank, kc=kc, qb=qb: e.matmul(
                psum[bank][:, :], W_sb[:, slv * 4096 + kc * 128:slv * 4096 + (kc + 1) * 128],
                W_sb[:, 16384 + slv * 2048 + qb * 512:16384 + slv * 2048 + (qb + 1) * 512],
                start=True, stop=True),
                incs=[(S_("c_sc"), 1)], waits=wts)

        pe_score(0)
        for i in range(NTL):
            h, qb, kc, nk = tiles[i]
            slv = h % 2
            bank = i % 2
            slot = i % 4
            masked = kc >= nk - 8
            wts = [("c_sc", i + 1)]
            if i == 0:
                wts.append(("bb_ve", BBVE))
            if i >= 4:
                wts.append((S_("c_pv"), i - 3))
            cx.op("scalar", lambda e, S, bank=bank, slot=slot: e.activation(
                out=probs[:, slot * 512:(slot + 1) * 512], in_=psum[bank][:, :],
                func=AF.Exp, bias=biascol[:, 0:1], scale=SCALE),
                incs=[(S_("c_exp"), 1)], waits=wts)
            C_EXP = cx.n("c_exp")
            if masked:
                m = kc - (nk - 8)
                mt = qb * 8 + m
                cx.op("gpsimd", lambda e, S, slot=slot, mt=mt: e.tensor_tensor(
                    probs[:, slot * 512:(slot + 1) * 512],
                    probs[:, slot * 512:(slot + 1) * 512],
                    mask_c[:, mt * 512:(mt + 1) * 512], ALU.mult),
                    incs=[(S_("c_msk"), 1)],
                    waits=[("c_exp", C_EXP)] + ([("c_mld", C_MLD)] if mcount == 0 else []))
                mcount += 1
            if i + 1 < NTL:
                pe_score(i + 1)
            swts = [("c_exp", C_EXP)]
            if masked:
                swts.append(("c_msk", cx.n("c_msk")))
            if kc == 0 and qtail >= 1:
                swts.append((S_("c_rcp"), qtail))
            cx.op("tensor", lambda e, S, slot=slot, st=(kc == 0), sp=(kc == nk - 1):
                  e.matmul(psum[3][0:1, :], onescol_sb[:, :],
                           probs[:, slot * 512:(slot + 1) * 512], start=st, stop=sp),
                  waits=swts)
            pvw = []
            if kc == 0:
                VH = khv[h][1]
                pvw.append(("c_vh%d" % slv, VH))
                if qtail >= 1:
                    pvw.append((S_("c_ynorm"), qtail))
            pvi = [(S_("c_pv"), 1)]
            cx.op("tensor", lambda e, S, slot=slot, slv=slv, kc=kc,
                  st=(kc == 0), sp=(kc == nk - 1): e.matmul(
                      psum[2][:, :],
                      W_sb[:, 8192 + slv * 4096 + kc * 128:8192 + slv * 4096 + (kc + 1) * 128],
                      probs[:, slot * 512:(slot + 1) * 512], start=st, stop=sp),
                  incs=pvi, waits=pvw)
            if kc == nk - 1:
                C_PV = cx.n("c_pv")
                cx.op("vector", lambda e, S: e.reciprocal(sumrow[0:1, :], psum[3][0:1, :]),
                      incs=[("c_rcp", 1)], waits=[("c_pv", C_PV)])
                cx.op("tensor", lambda e, S: e.matmul(
                    psum[4][:, :], onesfrow[:, :], sumrow[0:1, :], start=True, stop=True),
                    incs=[(S_("c_rbmm"), 1)],
                    waits=[("c_rcp", cx.n("c_rcp")), (S_("c_rbev"), qtail)])
                cx.op("vector", lambda e, S: e.tensor_copy(rbq[:, :], psum[4][:, :]),
                      incs=[("c_rbev", 1)], waits=[("c_rbmm", cx.n("c_rbmm"))])
                yoff = h * 2048 + qb * 512
                cx.op("vector", lambda e, S, yoff=yoff: e.tensor_tensor(
                    YB[:, yoff:yoff + 512], psum[2][:, :], rbq[:, :], ALU.mult),
                    incs=[("c_ynorm", 1)], waits=[("c_rbev", cx.n("c_rbev"))])
                qtail += 1

        C_YN = cx.n("c_ynorm")
        if "yall_d" in dbg:
            cx.op("gpsimd", lambda e, S: e.dma_start(out=yall_d, in_=YB[:, :]),
                  incs=[(S_("dbg_st"), 16)], waits=[("c_ynorm", C_YN)])

        if nph < 4:
            _finish(cx)
            _emit(nc, cx, SEM_NAMES, ctx)
            return nc

        attn_t = IB
        xq_f = IB.ap().bitcast(F32)
        C_YN_ALL = cx.n("c_ynorm")

        for cic in range(16):
            cx.op("gpsimd", lambda e, S, cic=cic: e.dma_start(
                out=W_sb[:, cic * 2048:(cic + 1) * 2048],
                in_=wo[cic * 128:(cic + 1) * 128, :]),
                incs=[(S_("d_wold"), 16)],
                waits=([("c_ynorm", C_YN_ALL)] if cic == 0 else []))
        cx.op("gpsimd", lambda e, S: e.dma_start(out=bo_sb[:, :], in_=bo),
              incs=[("d_wold", 16)])
        D_WOLD = cx.n("d_wold")

        atuse = [0, 0]
        atlast = [0, 0]
        sslast = [0, 0]
        for tt in range(16):
            sl = tt % 2
            for cb in range(4):
                gi = tt * 4 + cb
                bank = gi % 2
                wts = [("d_wold", D_WOLD)]
                if gi == 0:
                    wts.append(("c_ynorm", C_YN_ALL))
                if cx.n("d_ev") >= 2:
                    wts.append((S_("d_ev"), cx.n("d_ev") - 1))
                for i in range(16):
                    cx.op("tensor", lambda e, S, bank=bank, i=i, tt=tt, cb=cb:
                          e.matmul(psum[bank][:, :],
                                   YB[:, i * 2048 + tt * 128:i * 2048 + (tt + 1) * 128],
                                   W_sb[:, i * 2048 + cb * 512:i * 2048 + (cb + 1) * 512],
                                   start=(i == 0), stop=False),
                          waits=(wts if i == 0 else []))
                cx.op("tensor", lambda e, S, bank=bank, cb=cb: e.matmul(
                    psum[bank][:, :], onesrow_sb[:, :], bo_sb[0:1, cb * 512:(cb + 1) * 512],
                    start=False, stop=True),
                    incs=[(S_("d_pe"), 1)])
                D_PE = cx.n("d_pe")
                evw = [("d_pe", D_PE)]
                if atuse[sl]:
                    evw.append(("d_at%d" % sl, atlast[sl]))
                    evw.append((S_("d_ss"), sslast[sl]))
                cx.op("scalar", lambda e, S, bank=bank, sl=sl, cb=cb: e.activation(
                    out=attn_t[:, sl * 4096 + cb * 512:sl * 4096 + (cb + 1) * 512],
                    in_=psum[bank][:, :], func=AF.Copy),
                    incs=[(S_("d_ev"), 1)], waits=(evw if cb == 0 else [("d_pe", D_PE)]))
            D_EV = cx.n("d_ev")
            scrv = attn_t[:, (tt % 2) * 4096 + 8192:(tt % 2) * 4096 + 10240]
            cx.op("scalar", lambda e, S, sl=sl, scrv=scrv, tt=tt: e.activation(
                out=scrv, in_=attn_t[:, sl * 4096:sl * 4096 + 2048], func=AF.Copy,
                accum_out=stats[:, tt:tt + 1]),
                incs=[(S_("d_ss"), 1)])
            cx.op("scalar", lambda e, S, sl=sl, scrv=scrv, tt=tt: e.activation(
                out=scrv, in_=attn_t[:, sl * 4096:sl * 4096 + 2048], func=AF.Square,
                accum_out=stats[:, 128 + tt:129 + tt]),
                incs=[("d_ss", 1)])
            cx.op("gpsimd", lambda e, S, sl=sl, tt=tt: e.dma_start(
                out=attn_d[tt * 128:(tt + 1) * 128, :],
                in_=attn_t[:, sl * 4096:sl * 4096 + 2048]),
                incs=[(S_("d_at%d" % sl), 16)], waits=[("d_ss", cx.n("d_ss"))])
            atuse[sl] += 1
            atlast[sl] = cx.n("d_at%d" % sl)
            sslast[sl] = cx.n("d_ss")
        D_SS, D_PE_ALL = cx.n("d_ss"), cx.n("d_pe")

        def batch_cols(sum0, sq0, mu0, r0, nmr0, wsem, wval, isem):
            mu = stats[:, mu0:mu0 + 128]
            rr = stats[:, r0:r0 + 128]
            nmr = stats[:, nmr0:nmr0 + 128]
            tmp = stats[:, 768:896]
            lnv = stats[:, 896:1024]
            h1 = stag[:, 0:128]
            cx.op("vector", lambda e, S: e.scalar_tensor_tensor(
                mu, stats[:, sum0:sum0 + 128], 1.0 / C, stats[:, sum0:sum0 + 128],
                ALU.mult, ALU.bypass), waits=[(wsem, wval)])
            cx.op("vector", lambda e, S: e.scalar_tensor_tensor(
                tmp, stats[:, sq0:sq0 + 128], 1.0 / C, stats[:, sq0:sq0 + 128],
                ALU.mult, ALU.bypass))
            cx.op("vector", lambda e, S: e.tensor_tensor(nmr, mu, mu, ALU.mult))
            cx.op("vector", lambda e, S: e.tensor_tensor(tmp, tmp, nmr, ALU.subtract))
            cx.op("vector", lambda e, S: e.scalar_tensor_tensor(
                tmp, tmp, EPS, tmp, ALU.add, ALU.bypass),
                incs=[(S_("d_cmv"), 1)])
            cx.op("scalar", lambda e, S: e.activation(out=lnv, in_=tmp, func=AF.Ln),
                  incs=[(S_("d_cma"), 1)], waits=[("d_cmv", cx.n("d_cmv"))])
            cx.op("vector", lambda e, S: e.scalar_tensor_tensor(
                lnv, lnv, -0.5, lnv, ALU.mult, ALU.bypass),
                incs=[("d_cmv", 1)], waits=[("d_cma", cx.n("d_cma"))])
            cx.op("scalar", lambda e, S: e.activation(out=rr, in_=lnv, func=AF.Exp),
                  incs=[("d_cma", 1)], waits=[("d_cmv", cx.n("d_cmv"))])
            cx.op("vector", lambda e, S: e.tensor_tensor(h1, rr, rr, ALU.mult),
                  waits=[("d_cma", cx.n("d_cma"))])
            cx.op("vector", lambda e, S: e.tensor_tensor(h1, h1, tmp, ALU.mult))
            cx.op("vector", lambda e, S: e.scalar_tensor_tensor(
                h1, h1, -0.5, h1, ALU.mult, ALU.bypass))
            cx.op("vector", lambda e, S: e.scalar_tensor_tensor(
                h1, h1, 1.5, h1, ALU.add, ALU.bypass))
            cx.op("vector", lambda e, S: e.tensor_tensor(rr, rr, h1, ALU.mult))
            cx.op("vector", lambda e, S: e.tensor_tensor(nmr, mu, rr, ALU.mult))
            cx.op("vector", lambda e, S: e.scalar_tensor_tensor(
                nmr, nmr, -1.0, nmr, ALU.mult, ALU.bypass), incs=[(isem, 1)])

        batch_cols(0, 128, 256, 384, 512, "d_ss", D_SS, S_("d_cm"))
        D_CM = cx.n("d_cm")

        xquse = [0, 0]
        atl = [0, 0]
        x1use = [0, 0]
        x1last = [0, 0]
        for tt in range(16):
            sl = tt % 2
            xsl = S_("d_xq%d" % sl)
            asl = S_("d_atl%d" % sl)
            x1s = S_("d_x1s%d" % sl)
            gw = ([(S_("d_xsq"), cx.n("d_xsq") - 1)] if tt >= 2
                  else [("d_ss", D_SS)])
            cx.op("sync", lambda e, S, tt=tt, sl=sl: e.dma_start(
                out=xq_f[:, 4096 + sl * 2048:4096 + (sl + 1) * 2048],
                in_=xq[tt * 128:(tt + 1) * 128, :]),
                incs=[(xsl, 16)], waits=gw)
            cx.op("sync", lambda e, S, tt=tt, sl=sl: e.dma_start(
                out=attn_t[:, sl * 4096:sl * 4096 + 2048],
                in_=attn_d[tt * 128:(tt + 1) * 128, :]),
                incs=[(asl, 16)],
                waits=[("d_at%d" % sl, atlast[sl])] if atl[sl] == 0 else [])
            atl[sl] += 1
            XSL, ASL = cx.n(xsl), cx.n(asl)
            x1v = xq_f[:, 8192 + sl * 2048:8192 + (sl + 1) * 2048]
            w2 = [(xsl, XSL), (asl, ASL), ("d_cm", D_CM)]
            if x1use[sl]:
                w2.append((x1s, x1last[sl]))
            cx.op("vector", lambda e, S, sl=sl, tt=tt: e.scalar_tensor_tensor(
                xq_f[:, 4096 + sl * 2048:4096 + (sl + 1) * 2048],
                xq_f[:, 4096 + sl * 2048:4096 + (sl + 1) * 2048],
                stats[:, 512 + tt:513 + tt],
                xq_f[:, 4096 + sl * 2048:4096 + (sl + 1) * 2048],
                ALU.add, ALU.bypass), waits=w2)
            cx.op("vector", lambda e, S, sl=sl, tt=tt, x1v=x1v: e.scalar_tensor_tensor(
                x1v, attn_t[:, sl * 4096:sl * 4096 + 2048],
                stats[:, 384 + tt:385 + tt],
                xq_f[:, 4096 + sl * 2048:4096 + (sl + 1) * 2048],
                ALU.mult, ALU.add, accum_out=stats[:, 640 + tt:641 + tt]),
                incs=[(S_("d_x1"), 1)])
            cx.op("scalar", lambda e, S, sl=sl, tt=tt, x1v=x1v: e.activation(
                out=IB[:, 8192 + sl * 4096:8192 + sl * 4096 + 2048],
                in_=x1v, func=AF.Square,
                accum_out=stats[:, 768 + tt:769 + tt]),
                incs=[(S_("d_xsq"), 1)], waits=[("d_x1", cx.n("d_x1"))])
            cx.op("gpsimd", lambda e, S, tt=tt, x1v=x1v: e.dma_start(
                out=x1_d[tt * 128:(tt + 1) * 128, :], in_=x1v),
                incs=[(x1s, 16)], waits=[("d_x1", cx.n("d_x1"))])
            x1use[sl] += 1
            x1last[sl] = cx.n(x1s)
        D_X1 = cx.n("d_x1")

        cx.op("vector", None, waits=[("d_xsq", cx.n("d_xsq"))])
        batch_cols(640, 768, 0, 128, 256, "d_x1", D_X1, S_("d_cm2"))
        D_CM2 = cx.n("d_cm2")

        x1l = [0, 0]
        lnuse = [0, 0]
        lnlast = [0, 0]
        for tt in range(16):
            sl = tt % 2
            lsl = S_("d_x1l%d" % sl)
            lns = S_("d_ln%d" % sl)
            gw = [(S_("d_lnav"), cx.n("d_lnav") - 1)] if tt >= 2 else []
            cx.op("sync", lambda e, S, tt=tt, sl=sl: e.dma_start(
                out=xq_f[:, 4096 + sl * 2048:4096 + (sl + 1) * 2048],
                in_=x1_d[tt * 128:(tt + 1) * 128, :]),
                incs=[(lsl, 16)],
                waits=gw + ([("d_x1s%d" % sl, x1last[sl])] if x1l[sl] == 0 else []))
            x1l[sl] += 1
            LSL = cx.n(lsl)
            lnw = [(lsl, LSL), ("d_cm2", D_CM2)]
            if lnuse[sl]:
                lnw.append((lns, lnlast[sl]))
            cx.op("scalar", lambda e, S, sl=sl, tt=tt: e.activation(
                out=attn_t[:, sl * 4096:sl * 4096 + 2048],
                in_=xq_f[:, 4096 + sl * 2048:4096 + (sl + 1) * 2048],
                func=AF.Identity, scale=stats[:, 128 + tt:129 + tt],
                bias=stats[:, 256 + tt:257 + tt]),
                incs=[(S_("d_lnav"), 1)], waits=lnw)
            cx.op("gpsimd", lambda e, S, sl=sl, tt=tt: e.dma_start(
                out=ln2_d[tt * 128:(tt + 1) * 128, :],
                in_=attn_t[:, sl * 4096:sl * 4096 + 2048]),
                incs=[(lns, 16)], waits=[("d_lnav", cx.n("d_lnav"))])
            lnuse[sl] += 1
            lnlast[sl] = cx.n(lns)

        if nph < 5:
            _finish(cx)
            _emit(nc, cx, SEM_NAMES, ctx)
            return nc

        lnslab = IB
        ffn_t = IB[:, 8192:16384]
        silu_t = IB[:, 20480:20992]
        up_t = IB[:, 20992:21504]
        D_EV_ALL = cx.n("d_ev")
        D_LN0, D_LN1 = cx.n("d_ln0"), cx.n("d_ln1")
        ffsuse = [0, 0, 0, 0]
        ffslast = [0, 0, 0, 0]
        wguse = [0, 0]
        wwuse = [0, 0]
        wduse = [0, 0, 0, 0]

        for tp in range(4):
            lnw = ([(S_("e_pe"), cx.n("e_pe")), ("d_ss", cx.n("d_ss"))] if tp
                   else [("d_ln0", D_LN0), ("d_ln1", D_LN1), (S_("e_pe"), 0)])
            for ic in range(16):
                cx.op("sync", lambda e, S, tp=tp, ic=ic: e.dma_start_transpose(
                    lnslab[:, ic * 512:(ic + 1) * 512],
                    ln2_d[tp * 512:(tp + 1) * 512, ic * 128:(ic + 1) * 128]),
                    incs=[(S_("e_ln"), 16)], waits=(lnw if ic == 0 else []))
            E_LN = cx.n("e_ln")
            for w in range(32):
                wsl = w % 2
                wgw = [(S_("e_silu"), cx.n("e_silu") - 2)] if cx.n("e_silu") >= 2 else []
                if tp == 0 and w < 2:
                    wgw = wgw + [("d_ev", D_EV_ALL)]
                cx.op("sync", lambda e, S, tp=tp, w=w, wsl=wsl: e.dma_start(
                    out=W_sb[:, wsl * 4096:(wsl + 1) * 4096]
                        .rearrange("p (i o) -> p i o", o=256),
                    in_=wgT.rearrange("(i p) o -> p i o", p=128)[:, :, w * 256:(w + 1) * 256]),
                    incs=[(S_("e_wg%d" % wsl), 16)], waits=wgw)
                cx.op("sync", lambda e, S, tp=tp, w=w, wsl=wsl: e.dma_start(
                    out=W_sb[:, 8192 + wsl * 4096:8192 + (wsl + 1) * 4096]
                        .rearrange("p (i o) -> p i o", o=256),
                    in_=wuT.rearrange("(i p) o -> p i o", p=128)[:, :, w * 256:(w + 1) * 256]),
                    incs=[(S_("e_wu%d" % wsl), 16)])
                wguse[wsl] += 1
                WGW, WUW = cx.n("e_wg%d" % wsl), cx.n("e_wu%d" % wsl)
                for o2 in range(2):
                    ocg = w * 2 + o2
                    gb = o2 % 2
                    ub = 2 + o2 % 2
                    mmw = [("e_wg%d" % wsl, WGW), ("e_ln", E_LN)]
                    if cx.n("e_silu") >= 2:
                        mmw.append(("e_silu", cx.n("e_silu") - 1))
                    for ic in range(16):
                        cx.op("tensor", lambda e, S, gb=gb, wsl=wsl, ic=ic, o2=o2:
                              e.matmul(psum[gb][:, :],
                                       W_sb[:, wsl * 4096 + ic * 256 + o2 * 128:
                                            wsl * 4096 + ic * 256 + (o2 + 1) * 128],
                                       lnslab[:, ic * 512:(ic + 1) * 512],
                                       start=(ic == 0), stop=(ic == 15)),
                              waits=(mmw if ic == 0 else []))
                    mmw2 = [("e_wu%d" % wsl, WUW)]
                    if cx.n("e_up") >= 2:
                        mmw2.append(("e_up", cx.n("e_up") - 1))
                    for ic in range(16):
                        cx.op("tensor", lambda e, S, ub=ub, wsl=wsl, ic=ic, o2=o2:
                              e.matmul(psum[ub][:, :],
                                       W_sb[:, 8192 + wsl * 4096 + ic * 256 + o2 * 128:
                                            8192 + wsl * 4096 + ic * 256 + (o2 + 1) * 128],
                                       lnslab[:, ic * 512:(ic + 1) * 512],
                                       start=(ic == 0), stop=(ic == 15)),
                              incs=([(S_("e_pe"), 1)] if ic == 15 else []),
                              waits=(mmw2 if ic == 0 else []))
                    E_PE = cx.n("e_pe")
                    siw = [("e_pe", E_PE)]
                    if cx.n("e_int"):
                        siw.append((S_("e_int"), cx.n("e_int") - 0))
                    cx.op("scalar", lambda e, S, gb=gb: e.activation(
                        out=silu_t, in_=psum[gb][:, :], func=AF.Silu),
                        incs=[(S_("e_silu"), 1)], waits=siw)
                    cx.op("vector", lambda e, S, ub=ub: e.tensor_copy(
                        up_t, psum[ub][:, :]),
                        incs=[(S_("e_up"), 1)],
                        waits=[("e_pe", E_PE)] +
                              ([("e_int", cx.n("e_int"))] if cx.n("e_int") else []))
                    cx.op("gpsimd", lambda e, S, ocg=ocg: e.tensor_tensor(
                        YB[:, ocg * 512:(ocg + 1) * 512], silu_t, up_t, ALU.mult),
                        incs=[(S_("e_int"), 1)],
                        waits=[("e_silu", cx.n("e_silu")), ("e_up", cx.n("e_up"))])
            E_INT = cx.n("e_int")
            for half in range(2):
                for oc in range(64):
                    dsl = oc % 4
                    eocs = tp * 128 + half * 64 + oc
                    dw = [(S_("e_dmm"), cx.n("e_dmm") - 3)] if eocs >= 4 else []
                    cx.op("sync", lambda e, S, oc=oc, half=half, dsl=dsl: e.dma_start(
                        out=IB[:, 16384 + dsl * 1024:16384 + (dsl + 1) * 1024],
                        in_=wdT[oc * 128:(oc + 1) * 128,
                                half * 1024:(half + 1) * 1024]),
                        incs=[(S_("e_wd%d" % dsl), 16)], waits=dw)
                    wduse[dsl] += 1
                    WD = cx.n("e_wd%d" % dsl)
                    for tt in range(4):
                        for hb in range(2):
                            bank = tt * 2 + hb
                            w0 = []
                            if oc == 0:
                                w0 = [("e_int", E_INT), ("e_wd%d" % dsl, WD)]
                                if cx.n("e_ev"):
                                    w0.append((S_("e_ev"), cx.n("e_ev")))
                                if tp == 0 and half == 0:
                                    w0.append(("e_silu", cx.n("e_silu")))
                                    w0.append(("e_up", cx.n("e_up")))
                            else:
                                w0 = [("e_wd%d" % dsl, WD)] if (tt == 0 and hb == 0) else []
                            lastmm = (tt == 3 and hb == 1)
                            cx.op("tensor", lambda e, S, bank=bank, oc=oc, tt=tt, hb=hb, dsl=dsl:
                                  e.matmul(psum[bank][:, :],
                                           YB[:, oc * 512 + tt * 128:oc * 512 + (tt + 1) * 128],
                                           IB[:, 16384 + dsl * 1024 + hb * 512:
                                              16384 + dsl * 1024 + (hb + 1) * 512],
                                           start=(oc == 0), stop=(oc == 63)),
                                  incs=([(S_("e_dmm"), 1)] if lastmm else []),
                                  waits=w0)
                E_DMM = cx.n("e_dmm")
                for tt in range(4):
                    for hb in range(2):
                        bank = tt * 2 + hb
                        gt = tp * 4 + tt
                        evw = [("e_dmm", E_DMM)]
                        if ffsuse[tt]:
                            evw.append(("e_ffs%d" % tt, ffslast[tt]))
                        cx.op("scalar", lambda e, S, bank=bank, tt=tt, half=half, hb=hb:
                              e.activation(
                                  out=ffn_t[:, tt * 2048 + half * 1024 + hb * 512:
                                            tt * 2048 + half * 1024 + (hb + 1) * 512],
                                  in_=psum[bank][:, :], func=AF.Copy),
                              incs=[(S_("e_ev"), 1)], waits=evw)
            for tt in range(4):
                gt = tp * 4 + tt
                cx.op("scalar", lambda e, S, tt=tt, gt=gt: e.activation(
                    out=IB[:, 0:2048], in_=ffn_t[:, tt * 2048:(tt + 1) * 2048],
                    func=AF.Copy, accum_out=stats[:, 384 + gt:385 + gt]),
                    incs=[(S_("e_ss"), 1)], waits=[("e_ev", cx.n("e_ev"))])
                cx.op("scalar", lambda e, S, tt=tt, gt=gt: e.activation(
                    out=IB[:, 0:2048], in_=ffn_t[:, tt * 2048:(tt + 1) * 2048],
                    func=AF.Square, accum_out=stats[:, 512 + gt:513 + gt]),
                    incs=[("e_ss", 1)])
                cx.op("gpsimd", lambda e, S, tt=tt, gt=gt: e.dma_start(
                    out=ffn_d[gt * 128:(gt + 1) * 128, :],
                    in_=ffn_t[:, tt * 2048:(tt + 1) * 2048]),
                    incs=[(S_("e_ffs%d" % tt), 16)], waits=[("e_ss", cx.n("e_ss"))])
                ffsuse[tt] += 1
                ffslast[tt] = cx.n("e_ffs%d" % tt)
        E_SS = cx.n("e_ss")

        batch_cols(384, 512, 0, 128, 256, "e_ss", E_SS, S_("e_cm"))
        E_CM = cx.n("e_cm")

        x1f = W_sb.ap().bitcast(F32)
        x1l2 = [0, 0]
        outuse = [0, 0]
        outlast = [0, 0]
        for gt in range(16):
            sl = gt % 2
            xls = S_("e_x1l%d" % sl)
            fls = S_("e_ffl%d" % sl)
            ous = S_("e_out%d" % sl)
            gw = [(S_("e_fin"), cx.n("e_fin") - 1)] if gt >= 2 else \
                 [("e_pe", cx.n("e_pe")), ("e_dmm", cx.n("e_dmm"))]
            cx.op("sync", lambda e, S, gt=gt, sl=sl: e.dma_start(
                out=x1f[:, sl * 2048:(sl + 1) * 2048],
                in_=x1_d[gt * 128:(gt + 1) * 128, :]),
                incs=[(xls, 16)], waits=gw)
            cx.op("sync", lambda e, S, gt=gt, sl=sl: e.dma_start(
                out=W_sb[:, 8192 + sl * 2048:8192 + (sl + 1) * 2048],
                in_=ffn_d[gt * 128:(gt + 1) * 128, :]),
                incs=[(fls, 16)],
                waits=[("e_ffs%d" % (gt % 4), ffslast[gt % 4])] if gt < 2 else [])
            XL, FL = cx.n(xls), cx.n(fls)
            ow = [(xls, XL), (fls, FL), ("e_cm", E_CM)]
            if outuse[sl]:
                ow.append((ous, outlast[sl]))
            cx.op("vector", lambda e, S, sl=sl, gt=gt: e.scalar_tensor_tensor(
                x1f[:, sl * 2048:(sl + 1) * 2048],
                x1f[:, sl * 2048:(sl + 1) * 2048],
                stats[:, 256 + gt:257 + gt],
                x1f[:, sl * 2048:(sl + 1) * 2048], ALU.add, ALU.bypass),
                waits=ow)
            cx.op("vector", lambda e, S, sl=sl, gt=gt: e.scalar_tensor_tensor(
                x1f[:, 6144 + sl * 2048:6144 + (sl + 1) * 2048],
                W_sb[:, 8192 + sl * 2048:8192 + (sl + 1) * 2048],
                stats[:, 128 + gt:129 + gt],
                x1f[:, sl * 2048:(sl + 1) * 2048], ALU.mult, ALU.add),
                incs=[(S_("e_fin"), 1)])
            cx.op("gpsimd", lambda e, S, sl=sl, gt=gt: e.dma_start(
                out=out[gt * 128:(gt + 1) * 128, :],
                in_=x1f[:, 6144 + sl * 2048:6144 + (sl + 1) * 2048]),
                incs=[(ous, 16)], waits=[("e_fin", cx.n("e_fin"))])
            outuse[sl] += 1
            outlast[sl] = cx.n(ous)

        _finish(cx)
        _emit(nc, cx, SEM_NAMES, ctx)
        return nc

    return nc


def _finish(cx):
    gp = [n for n in cx.counts if n in ("a_st", "a_bnc", "rows_ld", "w_ld",
                                        "b_bld", "dbg_st", "d_wold")
          or n.startswith("b_ko")]
    for n in gp:
        cx.op("gpsimd", None, waits=[(n, cx.n(n))])
    sy = [n for n in cx.counts if n.startswith(("a_ld", "b_sl", "b_rr", "c_kh",
          "c_vh", "c_qh")) or n in ("set_ld", "c_mld")]
    for n in sy:
        cx.op("sync", None, waits=[(n, cx.n(n))])


def _emit(nc, cx, sem_names, ctx):
    S = {}
    for n in sorted(sem_names):
        S[n] = ctx.enter_context(nc.semaphore(n))
    block = ctx.enter_context(nc.Block())

    @block.tensor
    def _(e):
        for cmd in cx.cmds["tensor"]:
            cmd(e, S)

    @block.vector
    def _(e):
        for cmd in cx.cmds["vector"]:
            cmd(e, S)

    @block.scalar
    def _(e):
        for cmd in cx.cmds["scalar"]:
            cmd(e, S)

    @block.gpsimd
    def _(e):
        for cmd in cx.cmds["gpsimd"]:
            cmd(e, S)

    @block.sync
    def _(e):
        for cmd in cx.cmds["sync"]:
            cmd(e, S)


def _gen_weights(inputs):
    d5, gidx = spline_tables()
    cps = {"gcp": np.asarray(inputs["L0_ffn_gate_cp"], np.float32),
           "ucp": np.asarray(inputs["L0_ffn_up_cp"], np.float32),
           "dcp": np.asarray(inputs["L0_ffn_down_cp"], np.float32)}
    nc_gen = build_gen()
    in_maps = []
    for c in range(8):
        sl = slice(c * TPC, (c + 1) * TPC)
        in_maps.append({**cps,
                        "d5s": np.ascontiguousarray(d5[sl]),
                        "gidxT": np.ascontiguousarray(gidx[sl].T)})
    res = run_bass_kernel_spmd(nc_gen, in_maps, core_ids=list(range(8)))
    wg = np.concatenate([r["wg"] for r in res.results]).reshape(INTER, C)
    wu = np.concatenate([r["wu"] for r in res.results]).reshape(INTER, C)
    wd = np.concatenate([r["wd"] for r in res.results]).reshape(C, INTER)
    return wg, wu, wd


def fwd_in_maps(inputs, wg, wu, wd):
    x = np.asarray(inputs["L0_attn_x"], np.float32)
    wq = np.asarray(inputs["L0_attn_Wq"], np.float32)
    wk = np.asarray(inputs["L0_attn_Wk"], np.float32)
    wv = np.asarray(inputs["L0_attn_Wv"], np.float32)
    wo = np.asarray(inputs["L0_attn_Wo"], np.float32)
    bq = np.asarray(inputs["L0_attn_bq"], np.float32).reshape(1, H * D)
    bk = np.asarray(inputs["L0_attn_bk"], np.float32).reshape(1, H * D)
    bv = np.asarray(inputs["L0_attn_bv"], np.float32).reshape(1, H * D)
    bo = np.asarray(inputs["L0_attn_bo"], np.float32).reshape(1, C)
    wqkv = np.ascontiguousarray(np.stack([wq, wk, wv]))
    bqkv = np.ascontiguousarray(np.stack([bq, bk, bv]))
    wgT = np.ascontiguousarray(wg.T)
    wuT = np.ascontiguousarray(wu.T)
    wdT = np.ascontiguousarray(wd.T)
    masks = host_masks()
    ident = np.eye(128, dtype=np.float32)
    onesf = np.ones((1, 128), np.float32)
    onesb = np.ones((1, 128), BF)
    onescol = np.ones((128, 1), BF)
    maps = []
    for c in range(8):
        b, pi = divmod(c, 2)
        rows = _zigzag_rows(pi)
        maps.append({
            "xk": np.ascontiguousarray(x[b]),
            "xq": np.ascontiguousarray(x[b][rows]),
            "wqkv": wqkv, "wo": wo, "bqkv": bqkv, "bo": bo,
            "wgT": wgT, "wuT": wuT, "wdT": wdT,
            "masks": np.ascontiguousarray(masks[pi]),
            "identf": ident, "onesf": onesf, "onesb": onesb, "onescol": onescol,
        })
    return maps


def kernel(**inputs):
    wg, wu, wd = _gen_weights(inputs)
    nc_fwd = build_fwd()
    maps = fwd_in_maps(inputs, wg, wu, wd)
    res = run_bass_kernel_spmd(nc_fwd, maps, core_ids=list(range(8)))
    x = np.asarray(inputs["L0_attn_x"], np.float32)
    outp = np.empty((B, T, C), np.float32)
    for c in range(8):
        b, pi = divmod(c, 2)
        outp[b][_zigzag_rows(pi)] = res.results[c]["out"]
    return outp
